# version 01
# baseline (speedup 1.0000x reference)
import contextlib
import math

import numpy as np
import ml_dtypes

import concourse.bass as bass
import concourse.mybir as mybir
from concourse.bass_utils import run_bass_kernel_spmd

F32 = mybir.dt.float32
BF16 = mybir.dt.bfloat16
AF = mybir.ActivationFunctionType
ALU = mybir.AluOpType
AX = mybir.AxisListType
BF = ml_dtypes.bfloat16

NCORE = 8
S_FULL = 8192
TOK = S_FULL // NCORE
D = 2048
NCH = D // 128
DH = 1024
HID = 8192
NHEAD = 16
HDIM = 128
INV2PI = float(1.0 / (2.0 * math.pi))
TWOPI = float(2.0 * math.pi)
MAGIC = 8388608.0
ISQHD = float(1.0 / math.sqrt(HDIM))

ENGINES = ("sp", "act", "dve", "pe", "pool")


class Prog:

    NSLOT = 6

    def __init__(self, nc):
        self.nc = nc
        self.stack = contextlib.ExitStack()
        self.sems = {}
        for e in ENGINES:
            self.sems[(e, "c")] = self.stack.enter_context(nc.semaphore(f"s_{e}c"))
            for i in range(self.NSLOT):
                self.sems[(e, f"d{i}")] = self.stack.enter_context(
                    nc.semaphore(f"s_{e}d{i}"))
        self.count = {k: 0 for k in self.sems}
        self.dmaidx = {e: 0 for e in ENGINES}
        self.queue = {e: [] for e in ENGINES}
        self.last_writer = {}
        self.readers = {}
        self.satisfied = {e: {k: 0 for k in self.sems} for e in ENGINES}
        self.nops = 0
        self._consts = {}

    def sb(self, name, shape, dtype):
        return self.stack.enter_context(self.nc.sbuf_tensor(name, shape, dtype))

    def ps(self, name, shape, dtype=F32):
        return self.stack.enter_context(self.nc.psum_tensor(name, shape, dtype))

    def dram(self, name, shape, dtype=F32):
        return self.nc.dram_tensor(name, shape, dtype, kind="Internal").ap()

    def op(self, eng, fn, reads=(), writes=(), dma=False):
        if dma:
            slot = self.dmaidx[eng] % self.NSLOT
            self.dmaidx[eng] += 1
            semkey = (eng, f"d{slot}")
            inc = 16
        else:
            semkey = (eng, "c")
            inc = 1
        waits = {}

        def merge(d):
            for sk, c in d.items():
                if c > waits.get(sk, 0):
                    waits[sk] = c

        if dma and self.count[semkey] > 0:
            waits[semkey] = self.count[semkey]
        for k in reads:
            merge(self.last_writer.get(k, {}))
        for k in writes:
            merge(self.last_writer.get(k, {}))
            merge(self.readers.get(k, {}))
        sat = self.satisfied[eng]
        need = [(sk, c) for sk, c in waits.items() if c > sat[sk]]
        for sk, c in need:
            sat[sk] = c
        sem = self.sems[semkey]
        self.count[semkey] += inc
        sems = self.sems
        self.nops += 1

        def emit(engobj, need=tuple(need), fn=fn, sem=sem, inc=inc):
            for sk, c in need:
                engobj.wait_ge(sems[sk], c)
            fn(engobj).then_inc(sem, inc)

        self.queue[eng].append(emit)
        cnt = self.count[semkey]
        for k in writes:
            w = self.last_writer.setdefault(k, {})
            if cnt > w.get(semkey, 0):
                w[semkey] = cnt
            self.readers[k] = {}
        for k in reads:
            r = self.readers.setdefault(k, {})
            if cnt > r.get(semkey, 0):
                r[semkey] = cnt

    def phase(self):
        counts = {k: c for k, c in self.count.items() if c > 0}
        sems = self.sems
        for e in ENGINES:
            sat = self.satisfied[e]
            need = [(k, c) for k, c in counts.items() if c > sat[k]]
            for k, c in need:
                sat[k] = c
            if need:
                def bar(engobj, need=tuple(need)):
                    for k, c in need:
                        engobj.wait_ge(sems[k], c)
                self.queue[e].append(bar)
        self.last_writer = {}
        self.readers = {}

    def finish(self):
        counts = dict(self.count)
        sems = self.sems
        sat = self.satisfied["sp"]

        def fin(engobj):
            for key, c in counts.items():
                if c > 0 and c > sat.get(key, 0):
                    engobj.wait_ge(sems[key], c)

        self.queue["sp"].append(fin)
        secs = {"sp": "sync", "act": "scalar", "dve": "vector", "pe": "tensor",
                "pool": "gpsimd"}
        with self.nc.Block() as block:
            for e in ENGINES:
                if self.queue[e]:
                    getattr(block, secs[e])(
                        lambda engobj, q=tuple(self.queue[e]): [f(engobj) for f in q])

    def dma(self, eng, dst, src, rkey, wkey):
        self.op(eng, lambda E, d=dst, s=src: E.dma_start(out=d, in_=s),
                reads=_t(rkey), writes=_t(wkey), dma=True)

    def mm(self, out, lhsT, rhs, reads, writes, start=True, stop=True):
        self.op("pe", lambda E, o=out, l=lhsT, r=rhs, a=start, b=stop:
                E.matmul(o, l, r, start=a, stop=b),
                reads=_t(reads), writes=_t(writes))

    def cbias(self, value):
        if value not in self._consts:
            t = self.sb(f"cconst{len(self._consts)}", [128, 1], F32)
            self.memset(t[:, :], float(value), writes="consts")
            self._consts[value] = t
        return self._consts[value]

    def act(self, out, in_, func, reads, writes, bias=0.0, scale=1.0):
        reads = _t(reads)
        if isinstance(bias, float) and bias != 0.0:
            nparts = out.shape[0]
            bias = self.cbias(bias)[0:nparts, 0:1]
            reads = reads + ("consts",)
        self.op("act", lambda E, o=out, i=in_, f=func, b=bias, s=scale:
                E.activation(out=o, in_=i, func=f, bias=b, scale=s),
                reads=reads, writes=_t(writes))

    def tt(self, out, a, b, opk, reads, writes, eng="dve"):
        self.op(eng, lambda E, o=out, x=a, y=b, k=opk: E.tensor_tensor(o, x, y, k),
                reads=_t(reads), writes=_t(writes))

    def ts(self, out, in_, s1, s2, op0, op1, reads, writes, eng="dve"):
        self.op(eng, lambda E, o=out, i=in_, a=s1, b=s2, k0=op0, k1=op1:
                E.tensor_scalar(o, i, a, b, k0, k1),
                reads=_t(reads), writes=_t(writes))

    def stt(self, out, in0, scalar, in1, op0, op1, reads, writes, eng="dve"):
        self.op(eng, lambda E, o=out, i=in0, s=scalar, j=in1, k0=op0, k1=op1:
                E.scalar_tensor_tensor(o, i, s, j, k0, k1),
                reads=_t(reads), writes=_t(writes))

    def copy(self, out, in_, reads, writes, eng="dve"):
        self.op(eng, lambda E, o=out, i=in_: E.tensor_copy(o, i),
                reads=_t(reads), writes=_t(writes))

    def scan(self, out, d0, d1, init, reads, writes):
        self.op("dve", lambda E, o=out, a=d0, b=d1, i=init:
                E.tensor_tensor_scan(o, a, b, i, ALU.mult, ALU.add),
                reads=_t(reads), writes=_t(writes))

    def memset(self, ap, val, writes, eng="pool"):
        self.op(eng, lambda E, a=ap, v=val: E.memset(a, v), writes=_t(writes))

    def reduce(self, out, in_, opk, reads, writes, negate=False, eng="dve"):
        self.op(eng, lambda E, o=out, i=in_, k=opk, n=negate:
                E.tensor_reduce(o, i, AX.X, k, negate=n),
                reads=_t(reads), writes=_t(writes))


def _t(x):
    if isinstance(x, str):
        return (x,)
    return tuple(x)



def row_to_cols(p, row_ap, rkey, col_tile, ckey, pst, nchunk, ones, scale=1.0,
                off=0):
    for m in range(nchunk):
        p.mm(pst[:, m:m + 1], row_ap[0:1, m * 128:(m + 1) * 128], ones[0:1, 0:1],
             reads=(rkey, "ones"), writes=("rtc_ps",))
    p.act(col_tile[:, off:off + nchunk], pst[:, 0:nchunk], AF.Copy,
          reads=("rtc_ps",), writes=_t(ckey), scale=scale)


def emit_ln_affine(p, x, xkey, h, hkey, mrow, mkey, aoff, ones, tag, tok,
                  psA, psB, psC, fscr):
    fk = f"lnf{tag}"
    sq = fscr[:, 0:512]
    tmp = fscr[:, 512:1024]
    srow = fscr[0:1, 1024:1024 + 2 * tok]
    stat = fscr[0:1, 1024 + 2 * tok:1024 + 5 * tok]
    carow = fscr[0:1, 1024 + 5 * tok:1024 + 5 * tok + D]
    var = fscr[0:1, 1024 + 5 * tok + D:1024 + 6 * tok + D]
    sa = mrow[0:1, aoff:aoff + D]
    p.ts(carow, mrow[0:1, aoff + D:aoff + 2 * D], 1.0, None, ALU.add,
         ALU.bypass, reads=mkey, writes=fk)
    p.memset(stat[0:1, 2 * tok:3 * tok], 1.0, writes=fk)
    for n0 in range(0, tok, 512):
        for c in range(NCH):
            xc = x[:, c * tok + n0:c * tok + n0 + 512]
            p.mm(psA[0:1, :], ones[:, 0:1], xc, reads=(xkey, "ones"),
                 writes="lnpsA", start=(c == 0), stop=(c == NCH - 1))
            p.act(sq, xc, AF.Square, reads=xkey, writes=fk + "sq")
            p.mm(psB[0:1, :], ones[:, 0:1], sq, reads=(fk + "sq", "ones"),
                 writes="lnpsB", start=(c == 0), stop=(c == NCH - 1))
        p.act(srow[0:1, n0:n0 + 512], psA[0:1, :], AF.Copy, reads="lnpsA",
              writes=fk, scale=1.0 / D)
        p.act(srow[0:1, tok + n0:tok + n0 + 512], psB[0:1, :], AF.Copy,
              reads="lnpsB", writes=fk, scale=1.0 / D)
    mean = srow[0:1, 0:tok]
    msq = srow[0:1, tok:2 * tok]
    p.tt(var, mean, mean, ALU.mult, reads=fk, writes=fk + "v")
    p.tt(var, msq, var, ALU.subtract, reads=(fk, fk + "v"), writes=fk + "v")
    p.act(var, var, AF.Ln, reads=fk + "v", writes=fk + "v", bias=1e-5)
    p.act(stat[0:1, 0:tok], var, AF.Exp, reads=fk + "v", writes=fk, scale=-0.5)
    p.tt(stat[0:1, tok:2 * tok], mean, stat[0:1, 0:tok], ALU.mult,
         reads=fk, writes=fk)
    p.ts(stat[0:1, tok:2 * tok], stat[0:1, tok:2 * tok], -1.0, None, ALU.mult,
         ALU.bypass, reads=fk, writes=fk)
    for c in range(NCH):
        for n0 in range(0, tok, 512):
            p.mm(psC[:, :], carow[0:1, c * 128:(c + 1) * 128],
                 stat[0:1, n0:n0 + 512], reads=(fk,), writes="lnpsC")
            xc = x[:, c * tok + n0:c * tok + n0 + 512]
            hc = h[:, c * tok + n0:c * tok + n0 + 512]
            p.tt(tmp, xc, psC[:, :], ALU.mult, reads=(xkey, "lnpsC"),
                 writes=fk + "t")
            p.mm(psC[:, :], carow[0:1, c * 128:(c + 1) * 128],
                 stat[0:1, tok + n0:tok + n0 + 512], reads=(fk,),
                 writes="lnpsC", start=True, stop=False)
            p.mm(psC[:, :], sa[0:1, c * 128:(c + 1) * 128],
                 stat[0:1, 2 * tok + n0:2 * tok + n0 + 512],
                 reads=(mkey, fk), writes="lnpsC", start=False, stop=True)
            p.tt(hc, tmp, psC[:, :], ALU.add, reads=(fk + "t", "lnpsC"),
                 writes=hkey)



def build_mod(cols):
    nc = bass.Bass("TRN2", target_bir_lowering=False)
    c0 = nc.dram_tensor("c0", [1, D], F32, kind="ExternalInput").ap()
    c1 = nc.dram_tensor("c1", [1, D], F32, kind="ExternalInput").ap()
    w0 = nc.dram_tensor("w0", [D, cols], F32, kind="ExternalInput").ap()
    w1 = nc.dram_tensor("w1", [D, cols], F32, kind="ExternalInput").ap()
    o0 = nc.dram_tensor("o0", [1, cols], F32, kind="ExternalOutput").ap()
    o1 = nc.dram_tensor("o1", [1, cols], F32, kind="ExternalOutput").ap()
    p = Prog(nc)
    ct = p.sb("ct", [1, 2 * D], F32)
    st = p.sb("st", [1, 2 * D], F32)
    wt = p.sb("wt", [128, 2 * cols], F32)
    ot = p.sb("ot", [1, 2 * cols], F32)
    ones = p.sb("ones", [128, 1], F32)
    scols = p.sb("scols", [128, 32], F32)
    pst = p.ps("pst", [1, 512], F32)
    rtc_ps = p.ps("rtc_ps", [128, 16], F32)
    p.memset(ones[:, :], 1.0, writes="ones")
    p.dma("sp", ct[0:1, 0:D], c0, "c0", "ct")
    p.dma("sp", ct[0:1, D:2 * D], c1, "c1", "ct")
    p.act(st[0:1, :], ct[0:1, :], AF.Silu, reads="ct", writes="st")
    row_to_cols(p, st[0:1, 0:D], "st", scols, "scols", rtc_ps, 16, ones)
    row_to_cols(p, st[0:1, D:2 * D], "st", scols, "scols", rtc_ps, 16, ones,
                off=16)
    for li, w in ((0, w0), (1, w1)):
        co = li * cols
        for n0 in range(0, cols, 512):
            n1 = min(n0 + 512, cols)
            for k in range(NCH):
                wkey = f"wt{k % 2}"
                wb = wt[:, co + (k % 2) * 512:co + (k % 2) * 512 + (n1 - n0)]
                p.dma("sp", wb, w[k * 128:(k + 1) * 128, n0:n1], f"w{li}", wkey)
                p.mm(pst[0:1, 0:n1 - n0],
                     scols[:, li * 16 + k:li * 16 + k + 1],
                     wb, reads=("scols", wkey), writes="pst",
                     start=(k == 0), stop=(k == NCH - 1))
            p.act(ot[0:1, co + n0:co + n1], pst[0:1, 0:n1 - n0], AF.Copy,
                  reads="pst", writes="ot")
    p.dma("sp", o0, ot[0:1, 0:cols], "ot", "o0")
    p.dma("sp", o1, ot[0:1, cols:2 * cols], "ot", "o1")
    p.finish()
    return nc



def build_l0a():
    nc = bass.Bass("TRN2", target_bir_lowering=False)
    x_d = nc.dram_tensor("x", [D, TOK], F32, kind="ExternalInput").ap()
    mod_d = nc.dram_tensor("mod0", [1, 2 * D], F32, kind="ExternalInput").ap()
    win_d = nc.dram_tensor("win", [D, D], F32, kind="ExternalInput").ap()
    bin_d = nc.dram_tensor("binr", [1, D], F32, kind="ExternalInput").ap()
    wgx_d = nc.dram_tensor("wgx", [D, 128], F32, kind="ExternalInput").ap()
    wga_d = nc.dram_tensor("wga", [D, 128], F32, kind="ExternalInput").ap()
    bgx_d = nc.dram_tensor("bgx", [1, DH], F32, kind="ExternalInput").ap()
    bga_d = nc.dram_tensor("bga", [1, DH], F32, kind="ExternalInput").ap()
    ar_d = nc.dram_tensor("ar", [1, DH], F32, kind="ExternalInput").ap()
    ai_d = nc.dram_tensor("ai", [1, DH], F32, kind="ExternalInput").ap()
    m_o = nc.dram_tensor("m_o", [DH, TOK], F32, kind="ExternalOutput").ap()
    th_o = nc.dram_tensor("th_o", [DH, TOK], F32, kind="ExternalOutput").ap()
    xr_o = nc.dram_tensor("xr_o", [DH, TOK], F32, kind="ExternalOutput").ap()
    xi_o = nc.dram_tensor("xi_o", [DH, TOK], F32, kind="ExternalOutput").ap()

    p = Prog(nc)
    ones = p.sb("ones", [128, 1], F32)
    p.memset(ones[:, :], 1.0, writes="ones")
    mrow = p.sb("mrow", [1, 2 * D], F32)
    p.dma("sp", mrow[0:1, :], mod_d, "mod_d", "mrow")

    xa = p.sb("xa", [128, 2 * TOK], F32)
    h1 = p.sb("h1", [128, NCH * TOK], BF16)
    psA = p.ps("psA", [1, 512], F32)
    psB = p.ps("psB", [1, 512], F32)
    psC = p.ps("psC", [128, 512], F32)
    sq = p.sb("sqa", [128, 512], F32)
    srow = p.sb("srowa", [1, 2 * TOK], F32)
    stat = p.sb("stata", [1, 3 * TOK], F32)
    carow = p.sb("carowa", [1, D], F32)
    var = p.sb("vara", [1, TOK], F32)
    tmp = p.sb("lntmpa", [128, 512], F32)
    p.ts(carow[0:1, :], mrow[0:1, D:2 * D], 1.0, None, ALU.add, ALU.bypass,
         reads="mrow", writes="carow")
    p.memset(stat[0:1, 2 * TOK:3 * TOK], 1.0, writes="stat")
    for n0 in range(0, TOK, 512):
        for c in range(NCH):
            xb = xa[:, (c % 2) * TOK:(c % 2) * TOK + 512]
            xkey = f"xa{c % 2}"
            p.dma("sp", xb, x_d[c * 128:(c + 1) * 128, n0:n0 + 512], "x_d", xkey)
            p.mm(psA[0:1, :], ones[:, 0:1], xb, reads=(xkey, "ones"),
                 writes="psA", start=(c == 0), stop=(c == NCH - 1))
            p.act(sq[:, :], xb, AF.Square, reads=xkey, writes="sqa")
            p.mm(psB[0:1, :], ones[:, 0:1], sq[:, :], reads=("sqa", "ones"),
                 writes="psB", start=(c == 0), stop=(c == NCH - 1))
        p.act(srow[0:1, n0:n0 + 512], psA[0:1, :], AF.Copy, reads="psA",
              writes="srow", scale=1.0 / D)
        p.act(srow[0:1, TOK + n0:TOK + n0 + 512], psB[0:1, :], AF.Copy,
              reads="psB", writes="srow", scale=1.0 / D)
    mean = srow[0:1, 0:TOK]
    msq = srow[0:1, TOK:2 * TOK]
    p.tt(var[0:1, :], mean, mean, ALU.mult, reads="srow", writes="var")
    p.tt(var[0:1, :], msq, var[0:1, :], ALU.subtract, reads=("srow", "var"),
         writes="var")
    p.act(var[0:1, :], var[0:1, :], AF.Ln, reads="var", writes="var", bias=1e-5)
    p.act(stat[0:1, 0:TOK], var[0:1, :], AF.Exp, reads="var", writes="stat",
          scale=-0.5)
    p.tt(stat[0:1, TOK:2 * TOK], mean, stat[0:1, 0:TOK], ALU.mult,
         reads=("srow", "stat"), writes="stat")
    p.ts(stat[0:1, TOK:2 * TOK], stat[0:1, TOK:2 * TOK], -1.0, None, ALU.mult,
         ALU.bypass, reads="stat", writes="stat")
    for c in range(NCH):
        xb = xa[:, (c % 2) * TOK:(c % 2) * TOK + TOK]
        xkey = f"xa{c % 2}"
        p.dma("sp", xb, x_d[c * 128:(c + 1) * 128, :], "x_d", xkey)
        for n0 in range(0, TOK, 512):
            p.mm(psC[:, :], carow[0:1, c * 128:(c + 1) * 128],
                 stat[0:1, n0:n0 + 512], reads=("carow", "stat"), writes="psC")
            p.tt(tmp[:, :], xb[:, n0:n0 + 512], psC[:, :], ALU.mult,
                 reads=(xkey, "psC"), writes="lntmpa")
            p.mm(psC[:, :], carow[0:1, c * 128:(c + 1) * 128],
                 stat[0:1, TOK + n0:TOK + n0 + 512], reads=("carow", "stat"),
                 writes="psC", start=True, stop=False)
            p.mm(psC[:, :], mrow[0:1, c * 128:(c + 1) * 128],
                 stat[0:1, 2 * TOK + n0:2 * TOK + n0 + 512],
                 reads=("mrow", "stat"), writes="psC", start=False, stop=True)
            p.tt(h1[:, c * TOK + n0:c * TOK + n0 + 512], tmp[:, :], psC[:, :],
                 ALU.add, reads=("lntmpa", "psC"), writes="h1")

    winr = win_d.rearrange("(k p) m -> p k m", p=128)
    wbuf = p.sb("wbuf", [128, 2 * D], BF16)
    bcols = p.sb("bcols", [128, 16 + 8 + 8 + 8 + 8], F32)
    rtc_ps = p.ps("rtc_ps", [128, 16], F32)
    brow = p.sb("brow", [1, D], F32)
    onesrow = p.sb("onesrow", [1, 512], F32)
    p.memset(onesrow[0:1, :], 1.0, writes="onesrow")
    p.dma("sp", brow[0:1, :], bin_d, "bin_d", "binrow")
    xh = p.sb("xh", [128, NCH * TOK], BF16)
    psX = p.ps("psX", [128, 1024], F32)
    for m in range(NCH):
        wb = wbuf[:, (m % 2) * D:(m % 2) * D + D]
        wkey = f"wbuf{m % 2}"
        for kq in range(4):
            p.dma("pool",
                  wb.rearrange("p (k m) -> p k m", k=NCH)[:, 4 * kq:4 * kq + 4, :],
                  winr[:, 4 * kq:4 * kq + 4, m * 128:(m + 1) * 128],
                  "win_d", wkey)
        for n0 in (0, 512):
            p.mm(psX[:, n0:n0 + 512], brow[0:1, m * 128:(m + 1) * 128],
                 onesrow[0:1, :], reads=("binrow", "onesrow"), writes="psX",
                 start=True, stop=False)
            for k in range(NCH):
                p.mm(psX[:, n0:n0 + 512], wb[:, k * 128:(k + 1) * 128],
                     h1[:, k * TOK + n0:k * TOK + n0 + 512],
                     reads=(wkey, "h1"), writes="psX",
                     start=False, stop=(k == NCH - 1))
        p.act(xh[:, m * TOK:(m + 1) * TOK], psX[:, :], AF.Copy,
              reads="psX", writes="xh")

    rows2 = p.sb("rows2", [1, 4 * DH], F32)
    p.dma("sp", rows2[0:1, 0:DH], bgx_d, "bgx_d", "rows2")
    p.dma("sp", rows2[0:1, DH:2 * DH], bga_d, "bga_d", "rows2")
    p.dma("sp", rows2[0:1, 3 * DH:4 * DH], ai_d, "ai_d", "rows2")
    spr = p.sb("spr", [1, 3 * DH], F32)
    p.dma("sp", spr[0:1, 0:DH], ar_d, "ar_d", "spr")
    p.act(spr[0:1, DH:2 * DH], spr[0:1, 0:DH], AF.Abs, reads="spr", writes="spr")
    p.act(spr[0:1, DH:2 * DH], spr[0:1, DH:2 * DH], AF.Exp, reads="spr",
          writes="spr", scale=-1.0)
    p.act(spr[0:1, DH:2 * DH], spr[0:1, DH:2 * DH], AF.Ln, reads="spr",
          writes="spr", bias=1.0)
    p.act(spr[0:1, 2 * DH:3 * DH], spr[0:1, 0:DH], AF.Relu, reads="spr",
          writes="spr")
    p.tt(rows2[0:1, 2 * DH:3 * DH], spr[0:1, DH:2 * DH],
         spr[0:1, 2 * DH:3 * DH], ALU.add, reads="spr", writes="rows2")
    row_to_cols(p, rows2[0:1, 0:DH], "rows2", bcols, "bcols", rtc_ps, 8, ones,
                off=16)
    row_to_cols(p, rows2[0:1, DH:2 * DH], "rows2", bcols, "bcols", rtc_ps, 8,
                ones, off=24)
    row_to_cols(p, rows2[0:1, 2 * DH:3 * DH], "rows2", bcols, "bcols", rtc_ps,
                8, ones, off=32, scale=-8.0)
    row_to_cols(p, rows2[0:1, 3 * DH:4 * DH], "rows2", bcols, "bcols", rtc_ps,
                8, ones, off=40)

    wg2 = p.sb("wg2", [128, 2 * NCH * 128], BF16)
    for k in range(NCH):
        p.dma("pool", wg2[:, k * 128:(k + 1) * 128],
              wgx_d[k * 128:(k + 1) * 128, :], "wgx_d", "wg2")
        p.dma("pool", wg2[:, (NCH + k) * 128:(NCH + k + 1) * 128],
              wga_d[k * 128:(k + 1) * 128, :], "wga_d", "wg2")

    psG = p.ps("psG", [128, 1024], F32)
    gx = p.sb("gx", [128, TOK], F32)
    ga = p.sb("ga", [128, TOK], F32)
    la = p.sb("la", [128, TOK], F32)
    sc1 = p.sb("sc1", [128, TOK], F32)
    sc2 = p.sb("sc2", [128, TOK], F32)
    for b in range(8):
        for gi, (gt, gkey, boff) in enumerate(((gx, "gx", 16), (ga, "ga", 24))):
            woff = gi * NCH * 128
            for n0 in (0, 512):
                for kk in (2 * b, 2 * b + 1):
                    p.mm(psG[:, n0:n0 + 512],
                         wg2[:, woff + kk * 128:woff + (kk + 1) * 128],
                         xh[:, kk * TOK + n0:kk * TOK + n0 + 512],
                         reads=("wg2", "xh"), writes="psG",
                         start=(kk == 2 * b), stop=(kk == 2 * b + 1))
            p.act(gt[:, :], psG[:, :], AF.Sigmoid, reads=("psG", "bcols"),
                  writes=gkey, bias=bcols[:, boff + b:boff + b + 1])
        p.ts(la[:, :], ga[:, :], bcols[:, 32 + b:33 + b], None, ALU.mult,
             ALU.bypass, reads=("ga", "bcols"), writes="la")
        p.ts(sc1[:, :], ga[:, :], bcols[:, 40 + b:41 + b], None, ALU.mult,
             ALU.bypass, reads=("ga", "bcols"), writes="sc1")
        p.dma("sp", th_o[b * 128:(b + 1) * 128, :], sc1[:, :], "sc1", "th_o")
        p.act(sc2[:, :], la[:, :], AF.Exp, reads="la", writes="sc2")
        p.dma("sp", m_o[b * 128:(b + 1) * 128, :], sc2[:, :], "sc2", "m_o")
        p.act(sc2[:, :], la[:, :], AF.Exp, reads="la", writes="sc2", scale=2.0)
        p.act(sc2[:, :], sc2[:, :], AF.Ln, reads="sc2", writes="sc2",
              scale=-1.0, bias=1.0)
        p.act(sc2[:, :], sc2[:, :], AF.Exp, reads="sc2", writes="sc2",
              scale=0.5)
        p.tt(gx[:, :], gx[:, :], sc2[:, :], ALU.mult, reads=("gx", "sc2"),
             writes="gx")
        p.tt(sc1[:, :], gx[:, :], xh[:, b * TOK:(b + 1) * TOK], ALU.mult,
             reads=("gx", "xh"), writes="sc1")
        p.dma("sp", xr_o[b * 128:(b + 1) * 128, :], sc1[:, :], "sc1", "xr_o")
        p.tt(sc2[:, :], gx[:, :], xh[:, (8 + b) * TOK:(9 + b) * TOK], ALU.mult,
             reads=("gx", "xh"), writes="sc2")
        p.dma("sp", xi_o[b * 128:(b + 1) * 128, :], sc2[:, :], "sc2", "xi_o")
    p.finish()
    return nc



def build_scan():
    nc = bass.Bass("TRN2", target_bir_lowering=False)
    m_d = nc.dram_tensor("m_i", [128, S_FULL], F32, kind="ExternalInput").ap()
    th_d = nc.dram_tensor("th_i", [128, S_FULL], F32, kind="ExternalInput").ap()
    xr_d = nc.dram_tensor("xr_i", [128, S_FULL], F32, kind="ExternalInput").ap()
    xi_d = nc.dram_tensor("xi_i", [128, S_FULL], F32, kind="ExternalInput").ap()
    hr_o = nc.dram_tensor("hr_o", [128, S_FULL], BF16, kind="ExternalOutput").ap()
    hi_o = nc.dram_tensor("hi_o", [128, S_FULL], BF16, kind="ExternalOutput").ap()
    p = Prog(nc)
    SL = 2048
    mt = p.sb("mt", [128, SL], F32)
    tht = p.sb("tht", [128, SL], F32)
    xrt = p.sb("xrt", [128, SL], F32)
    xit = p.sb("xit", [128, SL], F32)
    sn = p.sb("sn", [128, SL], F32)
    cs = p.sb("cs", [128, SL], F32)
    rA = p.sb("rA", [128, SL], F32)
    t1 = p.sb("t1", [128, SL], F32)
    t2 = p.sb("t2", [128, SL], F32)
    ur = p.sb("ur", [128, SL], F32)
    ui = p.sb("ui", [128, SL], F32)
    hb = p.sb("hb", [128, 2 * SL], BF16)
    onescol = p.sb("onescol", [128, SL], F32)
    zc = p.sb("zc", [128, 3], F32)
    p.memset(onescol[:, :], 1.0, writes="ones")
    p.memset(zc[:, :], 0.0, writes="zc")
    for s in range(S_FULL // SL):
        o = s * SL
        p.dma("sp", mt[:, :], m_d[:, o:o + SL], "m_d", "mt")
        p.dma("sp", tht[:, :], th_d[:, o:o + SL], "th_d", "tht")
        p.dma("sp", xrt[:, :], xr_d[:, o:o + SL], "xr_d", "xrt")
        p.dma("sp", xit[:, :], xi_d[:, o:o + SL], "xi_d", "xit")
        p.scan(tht[:, :], onescol[:, :], tht[:, :], zc[:, 0:1],
               reads=("ones", "tht", "zc"), writes="tht")
        p.copy(zc[:, 0:1], tht[:, SL - 1:SL], reads="tht", writes="zc")
        p.ts(rA[:, :], tht[:, :], INV2PI, None, ALU.mult, ALU.bypass,
             reads="tht", writes="rA")
        p.ts(t1[:, :], rA[:, :], MAGIC, MAGIC, ALU.add, ALU.subtract,
             reads="rA", writes="t1")
        p.tt(rA[:, :], rA[:, :], t1[:, :], ALU.subtract, reads=("rA", "t1"),
             writes="rA")
        p.act(sn[:, :], rA[:, :], AF.Sin, reads="rA", writes="sn", scale=TWOPI)
        p.ts(rA[:, :], tht[:, :], INV2PI, 0.25, ALU.mult, ALU.add,
             reads="tht", writes="rA")
        p.ts(t1[:, :], rA[:, :], MAGIC, MAGIC, ALU.add, ALU.subtract,
             reads="rA", writes="t1")
        p.tt(rA[:, :], rA[:, :], t1[:, :], ALU.subtract, reads=("rA", "t1"),
             writes="rA")
        p.act(cs[:, :], rA[:, :], AF.Sin, reads="rA", writes="cs", scale=TWOPI)
        p.tt(t1[:, :], sn[:, :], xit[:, :], ALU.mult, reads=("sn", "xit"),
             writes="t1")
        p.tt(t2[:, :], sn[:, :], xrt[:, :], ALU.mult, reads=("sn", "xrt"),
             writes="t2")
        p.tt(xrt[:, :], cs[:, :], xrt[:, :], ALU.mult, reads=("cs", "xrt"),
             writes="xrt")
        p.tt(xit[:, :], cs[:, :], xit[:, :], ALU.mult, reads=("cs", "xit"),
             writes="xit")
        p.tt(xrt[:, :], xrt[:, :], t1[:, :], ALU.add, reads=("xrt", "t1"),
             writes="xrt")
        p.tt(xit[:, :], xit[:, :], t2[:, :], ALU.subtract, reads=("xit", "t2"),
             writes="xit")
        p.scan(ur[:, :], mt[:, :], xrt[:, :], zc[:, 1:2],
               reads=("mt", "xrt", "zc"), writes="ur")
        p.copy(zc[:, 1:2], ur[:, SL - 1:SL], reads="ur", writes="zc")
        p.scan(ui[:, :], mt[:, :], xit[:, :], zc[:, 2:3],
               reads=("mt", "xit", "zc"), writes="ui")
        p.copy(zc[:, 2:3], ui[:, SL - 1:SL], reads="ui", writes="zc")
        p.tt(t1[:, :], cs[:, :], ur[:, :], ALU.mult, reads=("cs", "ur"),
             writes="t1")
        p.tt(t2[:, :], sn[:, :], ui[:, :], ALU.mult, reads=("sn", "ui"),
             writes="t2")
        p.tt(hb[:, 0:SL], t1[:, :], t2[:, :], ALU.subtract, reads=("t1", "t2"),
             writes="hb")
        p.tt(t1[:, :], sn[:, :], ur[:, :], ALU.mult, reads=("sn", "ur"),
             writes="t1")
        p.tt(t2[:, :], cs[:, :], ui[:, :], ALU.mult, reads=("cs", "ui"),
             writes="t2")
        p.tt(hb[:, SL:2 * SL], t1[:, :], t2[:, :], ALU.add, reads=("t1", "t2"),
             writes="hb")
        p.dma("sp", hr_o[:, o:o + SL], hb[:, 0:SL], "hb", "hr_o")
        p.dma("sp", hi_o[:, o:o + SL], hb[:, SL:2 * SL], "hb", "hi_o")
    p.finish()
    return nc



def build_l0b():
    nc = bass.Bass("TRN2", target_bir_lowering=False)
    x_d = nc.dram_tensor("x", [D, TOK], F32, kind="ExternalInput").ap()
    h_d = nc.dram_tensor("h", [D, TOK], BF16, kind="ExternalInput").ap()
    wout_d = nc.dram_tensor("wout", [D, D], F32, kind="ExternalInput").ap()
    bout_d = nc.dram_tensor("boutr", [1, D], F32, kind="ExternalInput").ap()
    modr_d = nc.dram_tensor("modr", [1, 4 * D], F32, kind="ExternalInput").ap()
    wg_d = nc.dram_tensor("wg", [D, HID], F32, kind="ExternalInput").ap()
    wu_d = nc.dram_tensor("wu", [D, HID], F32, kind="ExternalInput").ap()
    wd_d = nc.dram_tensor("wd", [HID, D], F32, kind="ExternalInput").ap()
    x2_d = nc.dram_tensor("x2", [D, TOK], F32, kind="ExternalOutput").ap()
    hh_d = nc.dram_tensor("hh_s", [TOK, HID], F32, kind="Internal").ap()
    g_d = nc.dram_tensor("g_s", [TOK, HID], BF16, kind="Internal").ap()

    p = Prog(nc)
    ones = p.sb("ones", [128, 1], F32)
    onesrow = p.sb("onesrow", [1, 512], F32)
    p.memset(ones[:, :], 1.0, writes="ones")
    p.memset(onesrow[0:1, :], 1.0, writes="onesrow")
    xT = p.sb("xT", [128, NCH * TOK], F32)
    modsc = p.sb("modsc", [1, 2 * D], F32)
    bcols = p.sb("bcols", [128, 32], F32)
    aF = p.sb("aF", [128, 10240], F32)
    aB = p.sb("aB", [128, 40960], BF16)
    psA = p.ps("psA", [1, 512], F32)
    psB = p.ps("psB", [1, 512], F32)
    psC = p.ps("psC", [128, 512], F32)
    rtc = p.ps("rtc", [128, 16], F32)
    psW = p.ps("psW", [128, 1024], F32)
    psG = p.ps("psG", [128, 512], F32)
    psU = p.ps("psU", [128, 512], F32)

    grow = aF[0:1, 0:2 * D]
    t1r = aF[0:1, 2 * D:3 * D]
    t2r = aF[0:1, 3 * D:4 * D]
    p.dma("sp", grow[0:1, 0:D], modr_d[0:1, 0:D], "modr", "grow")
    p.dma("sp", grow[0:1, D:2 * D], modr_d[0:1, 3 * D:4 * D], "modr", "grow")
    p.dma("sp", modsc[0:1, :], modr_d[0:1, D:3 * D], "modr", "modsc")
    for gi in range(2):
        gseg = grow[0:1, gi * D:(gi + 1) * D]
        p.act(t1r, gseg, AF.Sign, reads="grow", writes="t1r")
        p.act(t2r, gseg, AF.Abs, reads="grow", writes="t2r")
        p.ts(t2r, t2r, 0.25, None, ALU.max, ALU.bypass, reads="t2r",
             writes="t2r")
        p.tt(gseg, t1r, t2r, ALU.mult, reads=("t1r", "t2r"), writes="grow")
        row_to_cols(p, gseg, "grow", bcols, "bcols", rtc, 16, ones,
                    off=gi * 16)
    for c in range(NCH):
        p.dma("sp", xT[:, c * TOK:(c + 1) * TOK],
              x_d[c * 128:(c + 1) * 128, :], "x_d", "xT")
    hbuf = aB[:, 0:NCH * TOK]
    for c in range(NCH):
        p.dma("sp", hbuf[:, c * TOK:(c + 1) * TOK],
              h_d[c * 128:(c + 1) * 128, :], "h_d", "hbuf")
    brow = aF[0:1, 4 * D:5 * D]
    p.dma("sp", brow, bout_d, "bout", "brow")
    p.phase()
    woutr = wout_d.rearrange("(k p) m -> p k m", p=128)
    yt = aF[:, 4096:5120]
    for m in range(NCH):
        wb = aB[:, NCH * TOK + (m % 2) * D:NCH * TOK + (m % 2) * D + D]
        wkey = f"wb{m % 2}"
        for kq in range(4):
            p.dma("pool",
                  wb.rearrange("p (k m) -> p k m", k=NCH)[:, 4 * kq:4 * kq + 4, :],
                  woutr[:, 4 * kq:4 * kq + 4, m * 128:(m + 1) * 128],
                  "wout_d", wkey)
        for n0 in (0, 512):
            p.mm(psW[:, n0:n0 + 512], brow[0:1, m * 128:(m + 1) * 128],
                 onesrow[0:1, :], reads=("brow", "onesrow"), writes="psW",
                 start=True, stop=False)
            for k in range(NCH):
                p.mm(psW[:, n0:n0 + 512], wb[:, k * 128:(k + 1) * 128],
                     hbuf[:, k * TOK + n0:k * TOK + n0 + 512],
                     reads=(wkey, "hbuf"), writes="psW",
                     start=False, stop=(k == NCH - 1))
        p.act(yt, psW[:, :], AF.Copy, reads="psW", writes="yt")
        p.stt(xT[:, m * TOK:(m + 1) * TOK], yt, bcols[:, m:m + 1],
              xT[:, m * TOK:(m + 1) * TOK], ALU.mult, ALU.add,
              reads=("yt", "bcols", "xT"), writes="xT")
    p.phase()
    emit_ln_affine(p, xT, "xT", hbuf, "hbuf", modsc, "modsc", 0, ones,
                  "ln2", TOK, psA, psB, psC, aF)
    p.phase()
    wgr = wg_d.rearrange("(k p) h -> p k h", p=128)
    wur = wu_d.rearrange("(k p) h -> p k h", p=128)
    sg = aF[:, 0:512]
    su = aF[:, 512:1024]
    shh = aF[:, 1024:1536]
    for ht in range(16):
        wgt = aB[:, NCH * TOK:NCH * TOK + 16 * 512]
        wut = aB[:, NCH * TOK + 16 * 512:NCH * TOK + 32 * 512]
        for kq in range(4):
            p.dma("pool",
                  wgt.rearrange("p (k h) -> p k h", h=512)[:, 4 * kq:4 * kq + 4, :],
                  wgr[:, 4 * kq:4 * kq + 4, ht * 512:(ht + 1) * 512],
                  "wg_d", "wgt")
            p.dma("pool",
                  wut.rearrange("p (k h) -> p k h", h=512)[:, 4 * kq:4 * kq + 4, :],
                  wur[:, 4 * kq:4 * kq + 4, ht * 512:(ht + 1) * 512],
                  "wu_d", "wut")
        for tt in range(8):
            for k in range(NCH):
                p.mm(psG[:, :], hbuf[:, k * TOK + tt * 128:k * TOK + (tt + 1) * 128],
                     wgt[:, k * 512:(k + 1) * 512], reads=("hbuf", "wgt"),
                     writes="psG", start=(k == 0), stop=(k == NCH - 1))
            for k in range(NCH):
                p.mm(psU[:, :], hbuf[:, k * TOK + tt * 128:k * TOK + (tt + 1) * 128],
                     wut[:, k * 512:(k + 1) * 512], reads=("hbuf", "wut"),
                     writes="psU", start=(k == 0), stop=(k == NCH - 1))
            p.act(sg, psG[:, :], AF.Silu, reads="psG", writes="sg")
            p.act(su, psU[:, :], AF.Copy, reads="psU", writes="su")
            p.tt(shh, sg, su, ALU.mult, reads=("sg", "su"), writes="shh")
            p.dma("sp", hh_d[tt * 128:(tt + 1) * 128, ht * 512:(ht + 1) * 512],
                  shh, "shh", "hh_d")
    p.phase()
    SEC = 2048
    hh = aF[:, 0:SEC]
    ha = aF[:, SEC:2 * SEC]
    s1 = aF[:, 2 * SEC:2 * SEC + SEC // 4]
    s2 = aF[:, 2 * SEC + SEC // 4:2 * SEC + SEC // 2]
    tq = aB[:, 0:6 * (SEC // 4)]
    gt = aB[:, 6 * (SEC // 4):6 * (SEC // 4) + SEC]
    Q = SEC // 4
    for tt in range(8):
        for sec in range(HID // SEC):
            p.dma("sp", hh, hh_d[tt * 128:(tt + 1) * 128,
                                  sec * SEC:(sec + 1) * SEC], "hh_d", "hh")
            p.act(ha, hh, AF.Abs, reads="hh", writes="ha")
            hq = [ha.rearrange("p (g f) -> p g f", f=4)[:, :, i] for i in range(4)]
            pairs = [(0, 1), (0, 2), (0, 3), (1, 2), (1, 3), (2, 3)]
            for pi, (a, b) in enumerate(pairs):
                p.tt(tq[:, pi * Q:(pi + 1) * Q], hq[a], hq[b], ALU.is_ge,
                     reads="ha", writes="tq")
            t01, t02, t03, t12, t13, t23 = [tq[:, i * Q:(i + 1) * Q]
                                            for i in range(6)]
            hhq = [hh.rearrange("p (g f) -> p g f", f=4)[:, :, i] for i in range(4)]
            gq = [gt.rearrange("p (g f) -> p g f", f=4)[:, :, i] for i in range(4)]
            p.tt(s1, t01, t02, ALU.add, reads="tq", writes="s1")
            p.tt(s1, s1, t03, ALU.add, reads=("s1", "tq"), writes="s1")
            p.stt(gq[0], s1, 1.5, hhq[0], ALU.is_ge, ALU.mult,
                  reads=("s1", "hh"), writes="gt")
            p.tt(s2, t12, t13, ALU.add, reads="tq", writes="s2")
            p.tt(s2, s2, t01, ALU.subtract, reads=("s2", "tq"), writes="s2")
            p.stt(gq[1], s2, 0.5, hhq[1], ALU.is_ge, ALU.mult,
                  reads=("s2", "hh"), writes="gt")
            p.tt(s1, t02, t12, ALU.add, reads="tq", writes="s1")
            p.tt(s1, s1, t23, ALU.subtract, reads=("s1", "tq"), writes="s1")
            p.stt(gq[2], s1, 0.5, hhq[2], ALU.is_le, ALU.mult,
                  reads=("s1", "hh"), writes="gt")
            p.tt(s2, t03, t13, ALU.add, reads="tq", writes="s2")
            p.tt(s2, s2, t23, ALU.add, reads=("s2", "tq"), writes="s2")
            p.stt(gq[3], s2, 1.5, hhq[3], ALU.is_le, ALU.mult,
                  reads=("s2", "hh"), writes="gt")
            p.dma("sp", g_d[tt * 128:(tt + 1) * 128, sec * SEC:(sec + 1) * SEC],
                  gt, "gt", "g_d")
    p.phase()
    wdr = wd_d.rearrange("(k p) d -> p k d", p=128)
    for th in range(2):
        gch = aB[:, 0:64 * 512]
        for k in range(64):
            p.op("sp", lambda E, dst=gch[:, k * 512:(k + 1) * 512],
                 src=g_d[th * 512:(th + 1) * 512, k * 128:(k + 1) * 128]:
                 E.dma_start(out=dst, in_=src, transpose=True),
                 reads=("g_d",), writes=("gch",), dma=True)
        for d in range(NCH):
            wdb = aB[:, 64 * 512:64 * 512 + HID]
            wkey = "wdb"
            for kq in range(8):
                p.dma("pool",
                      wdb.rearrange("p (k m) -> p k m", m=128)[:, 8 * kq:8 * kq + 8, :],
                      wdr[:, 8 * kq:8 * kq + 8, d * 128:(d + 1) * 128],
                      "wd_d", wkey)
            for k in range(64):
                p.mm(psW[:, 0:512], wdb[:, k * 128:(k + 1) * 128],
                     gch[:, k * 512:(k + 1) * 512], reads=(wkey, "gch"),
                     writes="psW", start=(k == 0), stop=(k == 63))
            p.stt(xT[:, d * TOK + th * 512:d * TOK + th * 512 + 512],
                  psW[:, 0:512], bcols[:, 16 + d:17 + d],
                  xT[:, d * TOK + th * 512:d * TOK + th * 512 + 512],
                  ALU.mult, ALU.add, reads=("psW", "bcols", "xT"), writes="xT")
    for c in range(NCH):
        p.dma("sp", x2_d[c * 128:(c + 1) * 128, :],
              xT[:, c * TOK:(c + 1) * TOK], "xT", "x2")
    p.finish()
    return nc



def build_l1a():
    nc = bass.Bass("TRN2", target_bir_lowering=False)
    x_d = nc.dram_tensor("x", [D, TOK], F32, kind="ExternalInput").ap()
    mods_d = nc.dram_tensor("mods", [1, 2 * D], F32, kind="ExternalInput").ap()
    ind4_d = nc.dram_tensor("ind4", [4, 128], F32, kind="ExternalInput").ap()
    cW = {}
    sW = {}
    bW = {}
    for w in ("q", "k", "v"):
        cW[w] = nc.dram_tensor(f"c{w}", [D, D], F32, kind="ExternalInput").ap()
        sW[w] = nc.dram_tensor(f"s{w}", [64, D], F32, kind="ExternalInput").ap()
        bW[w] = nc.dram_tensor(f"b{w}", [1, D], F32, kind="ExternalInput").ap()
    q_o = nc.dram_tensor("q_o", [D, TOK], BF16, kind="ExternalOutput").ap()
    k_o = nc.dram_tensor("k_o", [D, TOK], BF16, kind="ExternalOutput").ap()
    v_o = nc.dram_tensor("v_o", [TOK, D], BF16, kind="ExternalOutput").ap()
    qn_o = nc.dram_tensor("qn_o", [1, NHEAD], F32, kind="ExternalOutput").ap()
    kn_o = nc.dram_tensor("kn_o", [1, NHEAD], F32, kind="ExternalOutput").ap()

    p = Prog(nc)
    ones = p.sb("ones", [128, 1], F32)
    onesrow = p.sb("onesrow", [1, 512], F32)
    p.memset(ones[:, :], 1.0, writes="ones")
    p.memset(onesrow[0:1, :], 1.0, writes="onesrow")
    xT = p.sb("xT", [128, NCH * TOK], F32)
    modsc = p.sb("modsc", [1, 2 * D], F32)
    ind4 = p.sb("ind4t", [4, 128], F32)
    sc4 = p.sb("sc4", [4, NCH * 128], F32)
    nrow = p.sb("nrow", [1, 2 * NHEAD], F32)
    aF = p.sb("aF", [128, 11264], F32)
    aB = p.sb("aB", [128, 28672], BF16)
    psA = p.ps("psA", [1, 512], F32)
    psB = p.ps("psB", [1, 512], F32)
    psC = p.ps("psC", [128, 512], F32)
    psR = p.ps("psR", [128, 128], F32)
    psQ = p.ps("psQ", [128, 1024], F32)
    psV = p.ps("psV", [128, 128], F32)

    p.dma("sp", modsc[0:1, :], mods_d, "mods", "modsc")
    p.dma("sp", ind4[0:4, :], ind4_d, "ind4d", "ind4")
    for c in range(NCH):
        p.dma("sp", xT[:, c * TOK:(c + 1) * TOK],
              x_d[c * 128:(c + 1) * 128, :], "x_d", "xT")
    h3 = aB[:, 0:NCH * TOK]
    emit_ln_affine(p, xT, "xT", h3, "h3", modsc, "modsc", 0, ones, "ln3",
                  TOK, psA, psB, psC, aF)
    p.phase()
    qst = aB[:, NCH * TOK + 8192:NCH * TOK + 8192 + 1024]
    sqv = aF[:, 0:1024]
    nr2 = aF[0:1, 1024:1024 + 1024]
    for wi, w in enumerate(("q", "k", "v")):
        codes_r = cW[w].rearrange("(k p) m -> p k m", p=128)
        scales_r = sW[w].rearrange("(k j) m -> j k m", j=4)
        brow = aF[0:1, 2048:2048 + D]
        p.dma("sp", brow, bW[w], f"b{w}", "brow")
        for m in range(NCH):
            p.dma("sp", sc4[0:4, 0:2048].rearrange("j (k m) -> j k m", m=128),
                  scales_r[:, :, m * 128:(m + 1) * 128], f"s{w}", "sc4")
            cod = aB[:, NCH * TOK + (m % 2) * 2048:NCH * TOK + (m % 2) * 2048 + 2048]
            ckey = f"cod{m % 2}"
            wm = aB[:, NCH * TOK + 4096 + (m % 2) * 2048:
                    NCH * TOK + 4096 + (m % 2) * 2048 + 2048]
            wmkey = f"wm{m % 2}"
            for kq in range(4):
                p.dma("pool",
                      cod.rearrange("p (k m) -> p k m", m=128)[:, 4 * kq:4 * kq + 4, :],
                      codes_r[:, 4 * kq:4 * kq + 4, m * 128:(m + 1) * 128],
                      f"c{w}", ckey)
            for k in range(NCH):
                p.mm(psR[:, :], ind4[0:4, 0:128],
                     sc4[0:4, k * 128:(k + 1) * 128],
                     reads=("ind4", "sc4"), writes="psR")
                p.tt(wm[:, k * 128:(k + 1) * 128], cod[:, k * 128:(k + 1) * 128],
                     psR[:, :], ALU.mult, reads=(ckey, "psR"), writes=wmkey)
            if w in ("q", "k"):
                for n0 in (0, 512):
                    p.mm(psQ[:, n0:n0 + 512], brow[0:1, m * 128:(m + 1) * 128],
                         onesrow[0:1, :], reads=("brow", "onesrow"),
                         writes="psQ", start=True, stop=False)
                    for k in range(NCH):
                        p.mm(psQ[:, n0:n0 + 512], wm[:, k * 128:(k + 1) * 128],
                             h3[:, k * TOK + n0:k * TOK + n0 + 512],
                             reads=(wmkey, "h3"), writes="psQ",
                             start=False, stop=(k == NCH - 1))
                p.act(qst, psQ[:, :], AF.Copy, reads="psQ", writes="qst",
                      scale=(ISQHD if w == "q" else 1.0))
                p.dma("sp", (q_o if w == "q" else k_o)[m * 128:(m + 1) * 128, :],
                      qst, "qst", "qkout")
                p.act(sqv, qst, AF.Square, reads="qst", writes="sqv")
                for n0 in (0, 512):
                    p.mm(psA[0:1, :], ones[:, 0:1], sqv[:, n0:n0 + 512],
                         reads=("sqv", "ones"), writes="psA")
                    p.act(nr2[0:1, n0:n0 + 512], psA[0:1, :], AF.Copy,
                          reads="psA", writes="nr2")
                p.reduce(nrow[0:1, wi * NHEAD + m:wi * NHEAD + m + 1],
                         nr2[0:1, 0:1024], ALU.max, reads="nr2", writes="nrow")
            else:
                for tt in range(8):
                    p.mm(psV[:, :], onesrow[0:1, 0:128],
                         brow[0:1, m * 128:(m + 1) * 128],
                         reads=("onesrow", "brow"), writes="psV",
                         start=True, stop=False)
                    for k in range(NCH):
                        p.mm(psV[:, :],
                             h3[:, k * TOK + tt * 128:k * TOK + (tt + 1) * 128],
                             wm[:, k * 128:(k + 1) * 128],
                             reads=("h3", wmkey), writes="psV",
                             start=False, stop=(k == NCH - 1))
                    p.act(qst[:, 0:128], psV[:, :], AF.Copy, reads="psV",
                          writes="qst")
                    p.dma("sp", v_o[tt * 128:(tt + 1) * 128,
                                    m * 128:(m + 1) * 128],
                          qst[:, 0:128], "qst", "vout")
    p.dma("sp", qn_o, nrow[0:1, 0:NHEAD], "nrow", "qn_o")
    p.dma("sp", kn_o, nrow[0:1, NHEAD:2 * NHEAD], "nrow", "kn_o")
    p.finish()
    return nc



def build_attn():
    nc = bass.Bass("TRN2", target_bir_lowering=False)
    q_d = nc.dram_tensor("qh", [2 * HDIM, S_FULL], BF16, kind="ExternalInput").ap()
    k_d = nc.dram_tensor("kh", [2 * HDIM, S_FULL], BF16, kind="ExternalInput").ap()
    v_d = nc.dram_tensor("vh", [S_FULL, 2 * HDIM], BF16, kind="ExternalInput").ap()
    qn_d = nc.dram_tensor("qnm", [1, 16], F32, kind="ExternalInput").ap()
    kn_d = nc.dram_tensor("knm", [1, 16], F32, kind="ExternalInput").ap()
    ao_d = nc.dram_tensor("ao", [S_FULL, 2 * HDIM], BF16, kind="ExternalOutput").ap()

    p = Prog(nc)
    ones = p.sb("ones", [128, 1], F32)
    p.memset(ones[:, :], 1.0, writes="ones")
    rows = p.sb("rows", [1, 256], F32)
    negM = p.sb("negM", [128, 2], F32)
    qt = p.sb("qt", [128, S_FULL], BF16)
    kt = p.sb("kt", [128, S_FULL], BF16)
    vt = p.sb("vt", [128, 64 * 129], BF16)
    pT = p.sb("pT", [128, 2 * 512], BF16)
    ot = p.sb("ot", [128, 128], BF16)
    rec = p.sb("rec", [128, 1], F32)
    psS = [p.ps("psS0", [128, 512], F32), p.ps("psS1", [128, 512], F32)]
    psO = [p.ps(f"psO{i}", [128, 129], F32) for i in range(4)]
    psM = p.ps("psM", [128, 2], F32)

    p.dma("sp", rows[0:1, 0:16], qn_d, "qn_d", "rows")
    p.dma("sp", rows[0:1, 16:32], kn_d, "kn_d", "rows")
    for j in range(2):
        p.reduce(rows[0:1, 32 + j:33 + j], rows[0:1, j * 8:(j + 1) * 8],
                 ALU.max, reads="rows", writes="rows")
        p.reduce(rows[0:1, 34 + j:35 + j], rows[0:1, 16 + j * 8:16 + (j + 1) * 8],
                 ALU.max, reads="rows", writes="rows")
    p.tt(rows[0:1, 36:38], rows[0:1, 32:34], rows[0:1, 34:36], ALU.mult,
         reads="rows", writes="rows")
    p.act(rows[0:1, 36:38], rows[0:1, 36:38], AF.Ln, reads="rows", writes="rows")
    p.act(rows[0:1, 36:38], rows[0:1, 36:38], AF.Exp, reads="rows",
          writes="rows", scale=0.5)
    p.memset(rows[0:1, 64:192], 1.0, writes="rows")
    for j in range(2):
        p.mm(psM[:, j:j + 1], rows[0:1, 64:192], rows[0:1, 36 + j:37 + j],
             reads=("rows",), writes="psM")
    p.act(negM[:, :], psM[:, :], AF.Copy, reads="psM", writes="negM",
          scale=-1.0)
    p.memset(vt[:, :], 1.0, writes="vt")
    for hi in range(2):
        p.dma("sp", qt[:, :], q_d[hi * 128:(hi + 1) * 128, :], "q_d", "qt")
        p.dma("sp", kt[:, :], k_d[hi * 128:(hi + 1) * 128, :], "k_d", "kt")
        for kq in range(8):
            p.dma("sp",
                  vt[:, :].rearrange("p (k c) -> p k c", c=129)[:, 8 * kq:8 * kq + 8, 0:128],
                  v_d.rearrange("(k p) d -> p k d", p=128)[:, 8 * kq:8 * kq + 8,
                                                           hi * 128:(hi + 1) * 128],
                  "v_d", "vt")
        for qb in range(16):
            q0 = qb * 512
            for kt_i in range(64):
                ps = psS[kt_i % 2]
                pskey = f"psS{kt_i % 2}"
                p.mm(ps[:, :], kt[:, kt_i * 128:(kt_i + 1) * 128],
                     qt[:, q0:q0 + 512], reads=("kt", "qt"), writes=pskey)
                pb = pT[:, (kt_i % 2) * 512:(kt_i % 2) * 512 + 512]
                pbkey = f"pT{kt_i % 2}"
                p.op("act", lambda E, o=pb, i=ps[:, :], b=negM[:, hi:hi + 1]:
                     E.activation(out=o, in_=i, func=AF.Exp, bias=b, scale=1.0),
                     reads=(pskey, "negM"), writes=(pbkey,))
                for qs in range(4):
                    p.mm(psO[qs][:, :], pb[:, qs * 128:(qs + 1) * 128],
                         vt[:, kt_i * 129:kt_i * 129 + 129],
                         reads=(pbkey, "vt"), writes=f"psO{qs}",
                         start=(kt_i == 0), stop=(kt_i == 63))
            for qs in range(4):
                p.op("dve", lambda E, o=rec[:, 0:1], i=psO[qs][:, 128:129]:
                     E.reciprocal(o, i), reads=(f"psO{qs}",), writes=("rec",))
                p.ts(ot[:, :], psO[qs][:, 0:128], rec[:, 0:1], None, ALU.mult,
                     ALU.bypass, reads=(f"psO{qs}", "rec"), writes="ot")
                p.dma("sp", ao_d[q0 + qs * 128:q0 + (qs + 1) * 128,
                                 hi * 128:(hi + 1) * 128],
                      ot[:, :], "ot", "ao")
    p.finish()
    return nc



def build_l1b():
    nc = bass.Bass("TRN2", target_bir_lowering=False)
    x_d = nc.dram_tensor("x", [D, TOK], F32, kind="ExternalInput").ap()
    at_d = nc.dram_tensor("at", [D, TOK], BF16, kind="ExternalInput").ap()
    co_d = nc.dram_tensor("co", [D, D], F32, kind="ExternalInput").ap()
    so_d = nc.dram_tensor("so", [64, D], F32, kind="ExternalInput").ap()
    bo_d = nc.dram_tensor("bo", [1, D], F32, kind="ExternalInput").ap()
    ind4_d = nc.dram_tensor("ind4", [4, 128], F32, kind="ExternalInput").ap()
    modr_d = nc.dram_tensor("modr", [1, 4 * D], F32, kind="ExternalInput").ap()
    wg_d = nc.dram_tensor("wgt", [D, HID], F32, kind="ExternalInput").ap()
    wu_d = nc.dram_tensor("wut", [D, HID], F32, kind="ExternalInput").ap()
    wd_d = nc.dram_tensor("wdt", [HID, D], F32, kind="ExternalInput").ap()
    xo_d = nc.dram_tensor("xo", [D, TOK], F32, kind="ExternalOutput").ap()
    g_d = nc.dram_tensor("g1_s", [HID, TOK], BF16, kind="Internal").ap()

    p = Prog(nc)
    ones = p.sb("ones", [128, 1], F32)
    onesrow = p.sb("onesrow", [1, 512], F32)
    p.memset(ones[:, :], 1.0, writes="ones")
    p.memset(onesrow[0:1, :], 1.0, writes="onesrow")
    xT = p.sb("xT", [128, NCH * TOK], F32)
    modsc = p.sb("modsc", [1, 2 * D], F32)
    bcols = p.sb("bcols", [128, 32], F32)
    ind4 = p.sb("ind4t", [4, 128], F32)
    sc4 = p.sb("sc4", [4, NCH * 128], F32)
    aF = p.sb("aF", [128, 9216], F32)
    aB = p.sb("aB", [128, 40960], BF16)
    psA = p.ps("psA", [1, 512], F32)
    psB = p.ps("psB", [1, 512], F32)
    psC = p.ps("psC", [128, 512], F32)
    rtc = p.ps("rtc", [128, 16], F32)
    psW = p.ps("psW", [128, 1024], F32)
    psR = p.ps("psR", [128, 128], F32)

    grow = aF[0:1, 0:2 * D]
    t1r = aF[0:1, 2 * D:3 * D]
    t2r = aF[0:1, 3 * D:4 * D]
    p.dma("sp", grow[0:1, 0:D], modr_d[0:1, 0:D], "modr", "grow")
    p.dma("sp", grow[0:1, D:2 * D], modr_d[0:1, 3 * D:4 * D], "modr", "grow")
    p.dma("sp", modsc[0:1, :], modr_d[0:1, D:3 * D], "modr", "modsc")
    p.dma("sp", ind4[0:4, :], ind4_d, "ind4d", "ind4")
    for gi in range(2):
        gseg = grow[0:1, gi * D:(gi + 1) * D]
        p.act(t1r, gseg, AF.Sign, reads="grow", writes="t1r")
        p.act(t2r, gseg, AF.Abs, reads="grow", writes="t2r")
        p.ts(t2r, t2r, 0.25, None, ALU.max, ALU.bypass, reads="t2r",
             writes="t2r")
        p.tt(gseg, t1r, t2r, ALU.mult, reads=("t1r", "t2r"), writes="grow")
        row_to_cols(p, gseg, "grow", bcols, "bcols", rtc, 16, ones,
                    off=gi * 16)
    for c in range(NCH):
        p.dma("sp", xT[:, c * TOK:(c + 1) * TOK],
              x_d[c * 128:(c + 1) * 128, :], "x_d", "xT")
    abuf = aB[:, 0:NCH * TOK]
    for c in range(NCH):
        p.dma("sp", abuf[:, c * TOK:(c + 1) * TOK],
              at_d[c * 128:(c + 1) * 128, :], "at_d", "abuf")
    p.phase()
    brow = aF[0:1, 0:D]
    p.dma("sp", brow, bo_d, "bo_d", "brow")
    codes_r = co_d.rearrange("(k p) m -> p k m", p=128)
    scales_r = so_d.rearrange("(k j) m -> j k m", j=4)
    yt = aF[:, 2048:3072]
    for m in range(NCH):
        p.dma("sp", sc4[0:4, 0:2048].rearrange("j (k m) -> j k m", m=128),
              scales_r[:, :, m * 128:(m + 1) * 128], "so_d", "sc4")
        cod = aB[:, NCH * TOK + (m % 2) * 2048:NCH * TOK + (m % 2) * 2048 + 2048]
        ckey = f"cod{m % 2}"
        wm = aB[:, NCH * TOK + 4096 + (m % 2) * 2048:
                NCH * TOK + 4096 + (m % 2) * 2048 + 2048]
        wmkey = f"wm{m % 2}"
        for kq in range(4):
            p.dma("pool",
                  cod.rearrange("p (k m) -> p k m", m=128)[:, 4 * kq:4 * kq + 4, :],
                  codes_r[:, 4 * kq:4 * kq + 4, m * 128:(m + 1) * 128],
                  "co_d", ckey)
        for k in range(NCH):
            p.mm(psR[:, :], ind4[0:4, 0:128],
                 sc4[0:4, k * 128:(k + 1) * 128],
                 reads=("ind4", "sc4"), writes="psR")
            p.tt(wm[:, k * 128:(k + 1) * 128], cod[:, k * 128:(k + 1) * 128],
                 psR[:, :], ALU.mult, reads=(ckey, "psR"), writes=wmkey)
        for n0 in (0, 512):
            p.mm(psW[:, n0:n0 + 512], brow[0:1, m * 128:(m + 1) * 128],
                 onesrow[0:1, :], reads=("brow", "onesrow"), writes="psW",
                 start=True, stop=False)
            for k in range(NCH):
                p.mm(psW[:, n0:n0 + 512], wm[:, k * 128:(k + 1) * 128],
                     abuf[:, k * TOK + n0:k * TOK + n0 + 512],
                     reads=(wmkey, "abuf"), writes="psW",
                     start=False, stop=(k == NCH - 1))
        p.act(yt, psW[:, :], AF.Copy, reads="psW", writes="yt")
        p.stt(xT[:, m * TOK:(m + 1) * TOK], yt, bcols[:, m:m + 1],
              xT[:, m * TOK:(m + 1) * TOK], ALU.mult, ALU.add,
              reads=("yt", "bcols", "xT"), writes="xT")
    p.phase()
    emit_ln_affine(p, xT, "xT", abuf, "abuf", modsc, "modsc", 0, ones,
                  "ln4", TOK, psA, psB, psC, aF)
    p.phase()
    wgr = wg_d.rearrange("(k p) h -> p k h", p=128)
    wur = wu_d.rearrange("(k p) h -> p k h", p=128)
    sg = aF[:, 0:1024]
    su = aF[:, 1024:2048]
    ght = aB[:, NCH * TOK + 8192:NCH * TOK + 8192 + 1024]
    for hc in range(64):
        wgt = aB[:, NCH * TOK + (hc % 2) * 2048:NCH * TOK + (hc % 2) * 2048 + 2048]
        wut = aB[:, NCH * TOK + 4096 + (hc % 2) * 2048:
                 NCH * TOK + 4096 + (hc % 2) * 2048 + 2048]
        gkey, ukey = f"wgt{hc % 2}", f"wut{hc % 2}"
        for kq in range(4):
            p.dma("pool",
                  wgt.rearrange("p (k h) -> p k h", h=128)[:, 4 * kq:4 * kq + 4, :],
                  wgr[:, 4 * kq:4 * kq + 4, hc * 128:(hc + 1) * 128],
                  "wg_d", gkey)
            p.dma("pool",
                  wut.rearrange("p (k h) -> p k h", h=128)[:, 4 * kq:4 * kq + 4, :],
                  wur[:, 4 * kq:4 * kq + 4, hc * 128:(hc + 1) * 128],
                  "wu_d", ukey)
        for n0 in (0, 512):
            for k in range(NCH):
                p.mm(psW[:, n0:n0 + 512], wgt[:, k * 128:(k + 1) * 128],
                     abuf[:, k * TOK + n0:k * TOK + n0 + 512],
                     reads=(gkey, "abuf"), writes="psW",
                     start=(k == 0), stop=(k == NCH - 1))
            for k in range(NCH):
                p.mm(psC[:, :], wut[:, k * 128:(k + 1) * 128],
                     abuf[:, k * TOK + n0:k * TOK + n0 + 512],
                     reads=(ukey, "abuf"), writes="psC",
                     start=(k == 0), stop=(k == NCH - 1))
            p.act(sg[:, n0:n0 + 512], psW[:, n0:n0 + 512], AF.Silu,
                  reads="psW", writes="sg")
            p.act(su[:, n0:n0 + 512], psC[:, :], AF.Copy, reads="psC",
                  writes="su")
        p.tt(ght, sg, su, ALU.mult, reads=("sg", "su"), writes="ght")
        p.dma("sp", g_d[hc * 128:(hc + 1) * 128, :], ght, "ght", "g_d")
    p.phase()
    wdr = wd_d.rearrange("(k p) d -> p k d", p=128)
    for th in range(2):
        gch = aB[:, 0:64 * 512]
        for k in range(64):
            p.dma("sp", gch[:, k * 512:(k + 1) * 512],
                  g_d[k * 128:(k + 1) * 128, th * 512:(th + 1) * 512],
                  "g_d", "gch")
        for d in range(NCH):
            wdb = aB[:, 64 * 512:64 * 512 + HID]
            wkey = "wdb"
            for kq in range(8):
                p.dma("pool",
                      wdb.rearrange("p (k m) -> p k m", m=128)[:, 8 * kq:8 * kq + 8, :],
                      wdr[:, 8 * kq:8 * kq + 8, d * 128:(d + 1) * 128],
                      "wd_d", wkey)
            for k in range(64):
                p.mm(psW[:, 0:512], wdb[:, k * 128:(k + 1) * 128],
                     gch[:, k * 512:(k + 1) * 512], reads=(wkey, "gch"),
                     writes="psW", start=(k == 0), stop=(k == 63))
            p.stt(xT[:, d * TOK + th * 512:d * TOK + th * 512 + 512],
                  psW[:, 0:512], bcols[:, 16 + d:17 + d],
                  xT[:, d * TOK + th * 512:d * TOK + th * 512 + 512],
                  ALU.mult, ALU.add, reads=("psW", "bcols", "xT"), writes="xT")
    for c in range(NCH):
        p.dma("sp", xo_d[c * 128:(c + 1) * 128, :],
              xT[:, c * TOK:(c + 1) * TOK], "xT", "xo")
    p.finish()
    return nc



def _cc(a):
    return np.ascontiguousarray(a)


def kernel(**inp):
    CORES = list(range(NCORE))
    f32 = np.float32

    cols = 6 * D // NCORE
    nc = build_mod(cols)
    im = [{"c0": inp["L0_c"], "c1": inp["L1_c"],
           "w0": _cc(inp["L0_wmod"][:, r * cols:(r + 1) * cols]),
           "w1": _cc(inp["L1_wmod"][:, r * cols:(r + 1) * cols])}
          for r in range(NCORE)]
    res = run_bass_kernel_spmd(nc, im, core_ids=CORES)
    mod0 = np.concatenate([res.results[r]["o0"] for r in range(NCORE)], axis=1)
    mod1 = np.concatenate([res.results[r]["o1"] for r in range(NCORE)], axis=1)

    x_fm = _cc(inp["L0_attn_x"][0].astype(f32).T)

    nc = build_l0a()
    wgx2 = _cc(inp["L0_attn_Wg_x"].reshape(D, 128))
    wga2 = _cc(inp["L0_attn_Wg_a"].reshape(D, 128))
    mod0a = _cc(mod0[:, 0:2 * D])
    im = []
    for r in range(NCORE):
        ts_ = slice(r * TOK, (r + 1) * TOK)
        im.append({"x": _cc(x_fm[:, ts_]), "mod0": mod0a,
                   "win": inp["L0_attn_W_in"],
                   "binr": inp["L0_attn_b_in"][None, :],
                   "wgx": wgx2, "wga": wga2,
                   "bgx": inp["L0_attn_bg_x"][None, :],
                   "bga": inp["L0_attn_bg_a"][None, :],
                   "ar": inp["L0_attn_a_real_param"][None, :],
                   "ai": inp["L0_attn_a_imag_param"][None, :]})
    res = run_bass_kernel_spmd(nc, im, core_ids=CORES)
    m_f = np.concatenate([res.results[r]["m_o"] for r in range(NCORE)], axis=1)
    th_f = np.concatenate([res.results[r]["th_o"] for r in range(NCORE)], axis=1)
    xr_f = np.concatenate([res.results[r]["xr_o"] for r in range(NCORE)], axis=1)
    xi_f = np.concatenate([res.results[r]["xi_o"] for r in range(NCORE)], axis=1)

    nc = build_scan()
    im = []
    for r in range(NCORE):
        cs = slice(r * 128, (r + 1) * 128)
        im.append({"m_i": _cc(m_f[cs]), "th_i": _cc(th_f[cs]),
                   "xr_i": _cc(xr_f[cs]), "xi_i": _cc(xi_f[cs])})
    res = run_bass_kernel_spmd(nc, im, core_ids=CORES)
    hr_f = np.concatenate([np.asarray(res.results[r]["hr_o"]).view(BF)
                           for r in range(NCORE)], axis=0)
    hi_f = np.concatenate([np.asarray(res.results[r]["hi_o"]).view(BF)
                           for r in range(NCORE)], axis=0)
    h_fm = np.concatenate([hr_f, hi_f], axis=0)

    nc = build_l0b()
    modr0 = _cc(mod0[:, 2 * D:6 * D])
    im = []
    for r in range(NCORE):
        ts_ = slice(r * TOK, (r + 1) * TOK)
        im.append({"x": _cc(x_fm[:, ts_]), "h": _cc(h_fm[:, ts_]),
                   "wout": inp["L0_attn_W_out"],
                   "boutr": inp["L0_attn_b_out"][None, :], "modr": modr0,
                   "wg": inp["L0_ffn_w_gate"], "wu": inp["L0_ffn_w_up"],
                   "wd": inp["L0_ffn_w_down"]})
    res = run_bass_kernel_spmd(nc, im, core_ids=CORES)
    x2_fm = np.concatenate([res.results[r]["x2"] for r in range(NCORE)], axis=1)

    nc = build_l1a()
    ind4 = np.zeros((4, 128), f32)
    for j in range(4):
        ind4[j, j * 32:(j + 1) * 32] = 1.0
    mods1 = _cc(mod1[:, 0:2 * D])
    im = []
    for r in range(NCORE):
        ts_ = slice(r * TOK, (r + 1) * TOK)
        im.append({"x": _cc(x2_fm[:, ts_]), "mods": mods1, "ind4": ind4,
                   "cq": inp["L1_attn_wq_codes"], "sq": inp["L1_attn_wq_scales"],
                   "bq": inp["L1_attn_bq"][None, :],
                   "ck": inp["L1_attn_wk_codes"], "sk": inp["L1_attn_wk_scales"],
                   "bk": inp["L1_attn_bk"][None, :],
                   "cv": inp["L1_attn_wv_codes"], "sv": inp["L1_attn_wv_scales"],
                   "bv": inp["L1_attn_bv"][None, :]})
    res = run_bass_kernel_spmd(nc, im, core_ids=CORES)
    q_f = np.concatenate([np.asarray(res.results[r]["q_o"]).view(BF)
                          for r in range(NCORE)], axis=1)
    k_f = np.concatenate([np.asarray(res.results[r]["k_o"]).view(BF)
                          for r in range(NCORE)], axis=1)
    v_f = np.concatenate([np.asarray(res.results[r]["v_o"]).view(BF)
                          for r in range(NCORE)], axis=0)
    qn = np.stack([res.results[r]["qn_o"][0] for r in range(NCORE)])
    kn = np.stack([res.results[r]["kn_o"][0] for r in range(NCORE)])

    nc = build_attn()
    im = []
    for r in range(NCORE):
        hs = slice(2 * r * 128, (2 * r + 2) * 128)
        qnm = np.zeros((1, 16), f32)
        knm = np.zeros((1, 16), f32)
        for j in range(2):
            qnm[0, j * 8:(j + 1) * 8] = qn[:, 2 * r + j]
            knm[0, j * 8:(j + 1) * 8] = kn[:, 2 * r + j]
        im.append({"qh": _cc(q_f[hs]), "kh": _cc(k_f[hs]),
                   "vh": _cc(v_f[:, hs]), "qnm": qnm, "knm": knm})
    res = run_bass_kernel_spmd(nc, im, core_ids=CORES)
    ao = np.concatenate([np.asarray(res.results[r]["ao"]).view(BF)
                         for r in range(NCORE)], axis=1)

    nc = build_l1b()
    at_fm = _cc(ao.T)
    modr1 = _cc(mod1[:, 2 * D:6 * D])
    wgt = _cc(inp["L1_ffn_w_gate"].T)
    wut = _cc(inp["L1_ffn_w_up"].T)
    wdt = _cc(inp["L1_ffn_w_down"].T)
    im = []
    for r in range(NCORE):
        ts_ = slice(r * TOK, (r + 1) * TOK)
        im.append({"x": _cc(x2_fm[:, ts_]), "at": _cc(at_fm[:, ts_]),
                   "co": inp["L1_attn_wo_codes"], "so": inp["L1_attn_wo_scales"],
                   "bo": inp["L1_attn_bo"][None, :], "ind4": ind4,
                   "modr": modr1, "wgt": wgt, "wut": wut, "wdt": wdt})
    res = run_bass_kernel_spmd(nc, im, core_ids=CORES)
    x4_fm = np.concatenate([res.results[r]["xo"] for r in range(NCORE)], axis=1)
    return _cc(x4_fm.T)[None].astype(np.float32)
